# version 02 (final)
# baseline (speedup 1.0000x reference)
import numpy as np
import concourse.bass as bass
import concourse.mybir as mybir
import concourse.bass_utils as _bu
from concourse.bass_utils import run_bass_kernel_spmd

if not getattr(_bu, "_ldw_opt_patched", False):
    _orig_run_command = _bu.run_command

    def _run_command_ldw(argv, **kw):
        argv = ["--enable-ldw-opt=true" if a == "--enable-ldw-opt=false" else a
                for a in argv]
        return _orig_run_command(argv, **kw)

    _bu.run_command = _run_command_ldw
    _bu._ldw_opt_patched = True

F32 = mybir.dt.float32
BF16 = mybir.dt.bfloat16
AF = mybir.ActivationFunctionType
ALU = mybir.AluOpType
AX = mybir.AxisListType.X

D = 1024
G = 8
NPAIR = 8
TT = 512


class Ctr:
    def __init__(self, sem, step=1):
        self.sem = sem
        self.n = 0
        self.step = step


class Plan:
    ENGINES = ("pe", "act", "dve", "sp", "pool")

    def __init__(self):
        self.q = {k: [] for k in self.ENGINES}

    def run(self, eng_key, engine):
        for fn, waits, incs in self.q[eng_key]:
            for ctr, val in waits:
                if val > 0:
                    engine.wait_ge(ctr.sem, val)
            ins = fn(engine)
            for ctr, val in incs:
                ins.then_inc(ctr.sem, val)


def bc(ap_2d, reps):
    return ap_2d.unsqueeze(1).broadcast_to([ap_2d.shape[0], reps, ap_2d.shape[1]])


def build_nc(seq=8192, own=2048, debug=False, stages=9):
    import os
    from contextlib import ExitStack
    CVAR = set(os.environ.get("KVAR", "").split(","))

    nb1 = seq // TT
    nkc = seq // 128
    qw = [TT] * (own // TT) + [1]
    nq = len(qw)
    qoff = [sum(qw[:i]) for i in range(nq)]
    ownp1 = own + 1
    btiles = [(i * TT, TT) for i in range(own // TT)] + [(ownp1 - TT, TT)]

    nc = bass.Bass("TRN2", target_bir_lowering=False, debug=False,
                   detect_race_conditions=False)
    plan = Plan()

    xT_full = nc.dram_tensor("xT_full", [D, seq], F32, kind="ExternalInput").ap()
    xT_own = nc.dram_tensor("xT_own", [D, ownp1], F32, kind="ExternalInput").ap()
    W_in = {}
    for name in ("Wq", "Wk", "Wv", "Wo", "Wr_t", "Wk2_t", "Wv2_t"):
        W_in[name] = nc.dram_tensor(name, [D, D], F32, kind="ExternalInput").ap()
    vec_in = nc.dram_tensor("vec_pack", [128, 48], F32, kind="ExternalInput").ap()
    bv_in = nc.dram_tensor("bv_row", [1, D], F32, kind="ExternalInput").ap()
    mask_in = nc.dram_tensor("maskv", [128, 1], F32, kind="ExternalInput").ap()
    out_fm = nc.dram_tensor("out_fm", [D, own], F32, kind="ExternalOutput").ap()
    dbg = {}
    if debug:
        for nm, shp, dt in (("attn", [D, ownp1], BF16), ("qT", [D, ownp1], BF16),
                            ("kT", [NPAIR * 128, seq], BF16), ("v", [seq, D], BF16),
                            ("postattn", [D, ownp1], F32),
                            ("m_all", [128, ownp1], F32),
                            ("rstd_all", [128, ownp1], F32),
                            ("wkbf", [128, G * D], BF16),
                            ("ln1x", [128, G * TT], BF16),
                            ("mini", [128, NPAIR * 4], F32),
                            ("pmini", [128, NPAIR * 2 * 64], BF16)):
            dbg[nm] = nc.dram_tensor("dbg_" + nm, shp, dt, kind="ExternalOutput").ap()

    k_scr = nc.dram_tensor("k_scr", [NPAIR, 128, seq], BF16, kind="Internal").ap()
    v_scr = nc.dram_tensor("v_scr", [NPAIR, seq, 128], BF16, kind="Internal").ap()

    _probe = nc.alloc_sbuf_tensor("base_probe", [128, 8], F32)
    _BASE = nc.lookup_mloc(_probe).addr

    def A(name, shape, dtype, offset):
        return nc.alloc_sbuf_tensor_at(name, shape, dtype, offset=offset + _BASE)
    ones_bf = A("ones_bf", [128, 128], BF16, offset=0)
    vecp = A("vecp", [128, 48], F32, offset=256)
    consts = A("consts", [128, 8], F32, offset=448)
    ones_f = A("ones_f", [1, 128], F32, offset=480)
    mask_s = A("mask_s", [128, 1], F32, offset=992)
    bv_bc = A("bv_bc", [128, D], F32, offset=1024)
    qmax2 = A("qmax2", [128, NPAIR * nq], F32, offset=5120)
    mvec = A("mvec", [128, 8], F32, offset=5280)
    qT = A("qT", [128, NPAIR, ownp1], BF16, offset=5376)
    attno = A("attno", [128, NPAIR, ownp1], BF16, offset=38176)
    kmax2s = A("kmax2s", [128, NPAIR * nb1], F32, offset=209792)
    stat4 = A("stat4", [128, TT], F32, offset=210688)

    xf = [A("xf0", [128, G, TT], F32, offset=70976),
          A("xf1", [128, G, TT], F32, offset=87360)]
    x_bf = A("x_bf", [128, G, TT], BF16, offset=103744)
    x2_bf = A("x2_bf", [128, G, TT], BF16, offset=111936)
    ln1x = A("ln1x", [128, G, TT], BF16, offset=120128)
    Wq_bf = A("Wq_bf", [128, G, D], BF16, offset=128320)
    Wk_bf = A("Wk_bf", [128, G, D], BF16, offset=144704)
    Wv_bf = A("Wv_bf", [128, G, D], BF16, offset=161088)
    vtile = A("vtile", [128, 4, D], BF16, offset=177472)
    ksb = A("ksb", [128, 2, TT], BF16, offset=185664)
    k2tmp = A("k2tmp", [128, TT], BF16, offset=187712)
    m_s = A("m_s", [128, TT], F32, offset=188736)
    var_s = A("var_s", [128, TT], F32, offset=190784)
    rstd_s = A("rstd_s", [128, TT], F32, offset=192832)

    Kp = [A("Kp0", [128, seq], BF16, offset=70976),
          A("Kp1", [128, seq], BF16, offset=87360)]
    Vp = [A("Vp0", [128, nkc, 130], BF16, offset=103744),
          A("Vp1", [128, nkc, 130], BF16, offset=120384)]
    pring = A("pring", [128, 6, TT], BF16, offset=137024)
    rec_s = A("rec_s", [128, TT], F32, offset=143168)
    wcast = A("wcast", [128, 4, D], F32, offset=145216)
    Wo_bf = A("Wo_bf", [128, G, D], BF16, offset=161600)

    postattn = A("postattn", [128, G, ownp1], F32, offset=70976)
    xres = [A("xres0", [128, G, TT], F32, offset=136544),
            A("xres1", [128, G, TT], F32, offset=177984)]
    pa_bf = A("pa_bf", [128, G, TT], BF16, offset=5376)
    x2d_bf = A("x2d_bf", [128, G, TT], BF16, offset=13568)
    m_all = A("m_all", [128, ownp1], F32, offset=136544)
    rstd_all = A("rstd_all", [128, ownp1], F32, offset=144768)
    var2 = A("var2", [128, TT], F32, offset=194368)
    ln2t = A("ln2t", [128, G, TT + 1], F32, offset=5376)
    mixtmp = A("mixtmp", [128, TT], F32, offset=21792)
    r_ping = [A("r_p0", [128, TT], F32, offset=23840),
              A("r_p1", [128, TT], F32, offset=25888)]
    out_t2 = [A("outt0", [128, TT], F32, offset=27936),
              A("outt1", [128, TT], F32, offset=29984)]
    Wv2_bf = A("Wv2_bf", [128, G, D], BF16, offset=38176)
    kact = A("kact", [128, G, TT], BF16, offset=54560)
    xr_t = A("xr_t", [128, G, TT], BF16, offset=62752)
    xk_t = A("xk_t", [128, G, TT], BF16, offset=152992)
    Wr_bf = A("Wr_bf", [128, G, D], BF16, offset=161600)
    Wk2_bf = A("Wk2_bf", [128, G, D], BF16, offset=177984)

    sems = {k: nc.ctx.enter_context(nc.semaphore(k))
            for k in ("ld", "st", "pe", "act", "dve")}
    c_ld = Ctr(sems["ld"], 16)
    c_st = Ctr(sems["st"], 16)
    c_pe = Ctr(sems["pe"])
    c_act = Ctr(sems["act"])
    c_dve = Ctr(sems["dve"])

    def E(eng, fn, waits=(), inc=None):
        incs = ()
        n = None
        if inc is not None:
            inc.n += inc.step
            incs = ((inc, inc.step),)
            n = inc.n
        plan.q[eng].append((fn, tuple((c, v) for c, v in waits if v and v > 0), incs))
        return n

    def drain2(scr_ap_fn, waits=()):
        E("pe", lambda e: e.matmul(scr_ap_fn(), ones_bf[0:1, 0:1], ones_bf[0:1, 0:64],
                                   start=True, stop=True, skip_group_check=True),
          waits=waits)
        return E("pe", lambda e: e.matmul(scr_ap_fn(), ones_bf[0:1, 0:1],
                                          ones_bf[0:1, 0:64],
                                          start=True, stop=True, skip_group_check=True),
                 inc=c_pe)

    E("sp", lambda e: e.dma_start(out=vecp[:], in_=vec_in), inc=c_ld)
    n_mk = E("sp", lambda e: e.dma_start(out=mask_s[:], in_=mask_in), inc=c_ld)
    n_bv = E("sp", lambda e: e.dma_start(out=bv_bc[0:1, :], in_=bv_in), inc=c_ld)
    E("dve", lambda e: e.memset(ones_bf[:], 1.0), inc=c_dve)
    E("dve", lambda e: e.memset(ones_f[:], 1.0), inc=c_dve)
    n_ones = E("dve", lambda e: e.memset(consts[:], 1e-5), inc=c_dve)

    def cast_weight(dram_ap, dst_bf, stage_flat, wait_dve=0, wait_st=0):
        src3 = dram_ap.rearrange("(g p) d -> p g d", p=128)
        last = wait_dve
        for half in range(2):
            nl = E("sp", lambda e, s=src3, half=half: e.dma_start(
                out=stage_flat.rearrange("p (g d) -> p g d", d=D),
                in_=s[:, 4 * half:4 * half + 4, :]),
                waits=[(c_dve, last), (c_st, wait_st if half == 0 else 0)], inc=c_ld)
            last = E("dve", lambda e, d=dst_bf, half=half: e.tensor_copy(
                d[:, 4 * half:4 * half + 4, :].rearrange("p g d -> p (g d)"),
                stage_flat), waits=[(c_ld, nl)], inc=c_dve)
        return last

    def finish_early():
        fs = c_st.n
        E("pool", lambda e: e.dma_start(out=out_fm[0:128, 0:8],
                                        in_=vecp[:, 0:8]),
          waits=[(c_st, fs), (c_dve, c_dve.n), (c_act, c_act.n), (c_pe, c_pe.n),
                 (c_ld, c_ld.n)], inc=c_st)
        E("pool", lambda e: e.nop(), waits=[(c_st, c_st.n)])
        with nc.Block() as block:
            block.tensor(lambda e: plan.run("pe", e))
            block.scalar(lambda e: plan.run("act", e))
            block.vector(lambda e: plan.run("dve", e))
            block.sync(lambda e: plan.run("sp", e))
            block.gpsimd(lambda e: plan.run("pool", e))
        return nc

    xf0_flat = xf[0][:].rearrange("p a t -> p (a t)")
    n_c1 = cast_weight(W_in["Wq"], Wq_bf, xf0_flat)
    n_c2 = cast_weight(W_in["Wk"], Wk_bf, xf0_flat, wait_dve=n_c1)
    n_wqkv = cast_weight(W_in["Wv"], Wv_bf, xf0_flat, wait_dve=n_c2)

    with nc.psum_tensor("ps_bv", [128, TT], F32) as ps_bv:
        ncp = 0
        for h in range(2):
            nmm = E("pe", lambda e, h=h: e.matmul(
                ps_bv[:], ones_f[0:1, :], bv_bc[0:1, h * TT:(h + 1) * TT],
                start=True, stop=True, skip_group_check=True),
                waits=[(c_ld, n_bv), (c_dve, n_ones), (c_dve, ncp)], inc=c_pe)
            ncp = E("dve", lambda e, h=h: e.tensor_copy(
                bv_bc[:, h * TT:(h + 1) * TT], ps_bv[:]),
                waits=[(c_pe, nmm)], inc=c_dve)

    if stages <= 0:
        return finish_early()

    def ln_tile(src_cols_ap, w, xslot, xfree_dve, xbf_free_pe, ln_free_pe,
                ps_mean, ps_msq, m_dst=None, r_dst=None):
        src3 = src_cols_ap.rearrange("(g p) t -> p g t", p=128)
        nl = E("sp", lambda e, s=src3, x=xslot, w=w: e.dma_start(
            out=x[:, :, 0:w], in_=s, allow_slow_non_contiguous=(w < 32)),
            waits=[(c_dve, xfree_dve)], inc=c_ld)
        xin = xslot[:, :, 0:w]
        ncast = E("dve", lambda e, xin=xin, w=w: e.tensor_copy(x_bf[:, :, 0:w], xin),
                  waits=[(c_ld, nl), (c_pe, xbf_free_pe)], inc=c_dve)
        nsq = E("dve", lambda e, xin=xin, w=w: e.tensor_tensor(
            x2_bf[:, :, 0:w], xin, xin, ALU.mult), inc=c_dve)
        nm1 = nm2 = None
        for g in range(G):
            nm1 = E("pe", lambda e, g=g, w=w: e.matmul(
                ps_mean[:, 0:w], ones_bf[:], x_bf[:, g, 0:w],
                start=(g == 0), stop=(g == G - 1), skip_group_check=True),
                waits=[(c_dve, ncast)] if g == 0 else (),
                inc=c_pe if g == G - 1 else None)
        for g in range(G):
            nm2 = E("pe", lambda e, g=g, w=w: e.matmul(
                ps_msq[:, 0:w], ones_bf[:], x2_bf[:, g, 0:w],
                start=(g == 0), stop=(g == G - 1), skip_group_check=True),
                waits=[(c_dve, nsq)] if g == 0 else (),
                inc=c_pe if g == G - 1 else None)
        if w < 32:
            nm1 = nm2 = drain2(lambda: ps_k[0][0:1, 320:384])
        E("dve", lambda e, w=w: e.tensor_scalar_mul(m_s[:, 0:w], ps_mean[:, 0:w], 1.0 / D),
          waits=[(c_pe, nm1)], inc=c_dve)
        E("dve", lambda e, w=w: e.tensor_mul(stat4[:, 0:w], m_s[:, 0:w], m_s[:, 0:w]),
          inc=c_dve)
        nvar = E("dve", lambda e, w=w: e.scalar_tensor_tensor(
            var_s[:, 0:w], ps_msq[:, 0:w], 1.0 / D, stat4[:, 0:w], ALU.mult, ALU.subtract),
            waits=[(c_pe, nm2)], inc=c_dve)
        if m_dst is not None:
            E("dve", lambda e, w=w, md=m_dst: e.tensor_copy(md, m_s[:, 0:w]), inc=c_dve)
        E("act", lambda e, w=w: e.activation(stat4[:, 0:w], var_s[:, 0:w], AF.Ln,
                                             bias=consts[:, 0:1]),
          waits=[(c_dve, nvar)], inc=c_act)
        nrstd = E("act", lambda e, w=w: e.activation(rstd_s[:, 0:w], stat4[:, 0:w],
                                                     AF.Exp, scale=-0.5), inc=c_act)
        if r_dst is not None:
            E("dve", lambda e, w=w, rd=r_dst: e.tensor_copy(rd, rstd_s[:, 0:w]),
              waits=[(c_act, nrstd)], inc=c_dve)
        E("dve", lambda e, xin=xin, w=w: e.tensor_tensor(
            xin, xin, bc(m_s[:, 0:w], G), ALU.subtract),
            waits=[(c_act, nrstd)], inc=c_dve)
        napl = E("dve", lambda e, xin=xin, w=w: e.tensor_tensor(
            ln1x[:, :, 0:w], xin, bc(rstd_s[:, 0:w], G), ALU.mult),
            waits=[(c_pe, ln_free_pe)], inc=c_dve)
        return napl

    ps_stack = ExitStack()
    ps_mean = ps_stack.enter_context(nc.psum_tensor("ps_mean", [128, TT], F32))
    ps_msq = ps_stack.enter_context(nc.psum_tensor("ps_msq", [128, TT], F32))
    ps_k = [ps_stack.enter_context(nc.psum_tensor(f"ps_k{i}", [128, TT], F32))
            for i in range(2)]
    ps_v = [ps_stack.enter_context(nc.psum_tensor(f"ps_v{i}", [128, D], F32))
            for i in range(2)]

    xf_free = [n_wqkv, n_wqkv]
    ksb_free = [0, 0]
    ps_k_rmax = [0, 0]
    ps_k_evac = [0, 0]
    ps_v_free = [0, 0]
    vt_written = 0
    last_pe_of_tile = 0

    for t in range(nb1):
        sl = t % 2
        napl = ln_tile(xT_full[:, t * TT:(t + 1) * TT], TT, xf[sl], xf_free[sl],
                       last_pe_of_tile, last_pe_of_tile, ps_mean, ps_msq)
        xf_free[sl] = napl
        for p in range(NPAIR):
            kb = p % 2
            wl = ([(c_dve, napl)] if p <= 1 else []) + \
                 [(c_act, ps_k_evac[kb]), (c_dve, ps_k_rmax[kb])]
            nkmm = None
            for g in range(G):
                nkmm = E("pe", lambda e, g=g, p=p, kb=kb: e.matmul(
                    ps_k[kb][:], Wk_bf[:, g, p * 128:(p + 1) * 128], ln1x[:, g, :],
                    start=(g == 0), stop=(g == G - 1), skip_group_check=True),
                    waits=wl if g == 0 else (), inc=c_pe if g == G - 1 else None)
            nkev = E("act", lambda e, p=p, kb=kb: e.activation(
                ksb[:, kb, :], ps_k[kb][:], AF.Identity, bias=vecp[:, 8 + p:9 + p]),
                waits=[(c_pe, nkmm), (c_st, ksb_free[kb])], inc=c_act)
            ps_k_evac[kb] = nkev
            ksb_free[kb] = E("pool", lambda e, p=p, t=t, kb=kb: e.dma_start(
                out=k_scr[p, :, t * TT:(t + 1) * TT], in_=ksb[:, kb, :]),
                waits=[(c_act, nkev)], inc=c_st)
            nk2 = E("dve", lambda e, kb=kb: e.tensor_tensor(
                k2tmp[:], ksb[:, kb, :], ksb[:, kb, :], ALU.mult),
                waits=[(c_act, nkev)], inc=c_dve)
            nkmx = E("pe", lambda e, kb=kb: e.matmul(
                ps_k[kb][:], ones_bf[:], k2tmp[:], start=True, stop=True,
                skip_group_check=True), waits=[(c_dve, nk2)], inc=c_pe)
            ps_k_rmax[kb] = E("dve", lambda e, p=p, t=t, kb=kb: e.reduce_max(
                kmax2s[:, p * nb1 + t:p * nb1 + t + 1], ps_k[kb][:], axis=AX),
                waits=[(c_pe, nkmx)], inc=c_dve)
        for tc in range(4):
            vb = tc % 2
            wl = [(c_dve, ps_v_free[vb])]
            nvmm = None
            first = True
            for g in range(G):
                for h in range(2):
                    nvmm = E("pe", lambda e, g=g, h=h, tc=tc, vb=vb: e.matmul(
                        ps_v[vb][:, h * TT:(h + 1) * TT],
                        ln1x[:, g, tc * 128:(tc + 1) * 128],
                        Wv_bf[:, g, h * TT:(h + 1) * TT],
                        start=(g == 0), stop=(g == G - 1), skip_group_check=True),
                        waits=wl if first else (),
                        inc=c_pe if (g == G - 1 and h == 1) else None)
                    first = False
            wl3 = [(c_pe, nvmm)] + ([(c_st, vt_written)] if tc == 0 else [])
            ps_v_free[vb] = E("dve", lambda e, tc=tc, vb=vb: e.tensor_tensor(
                vtile[:, tc, :], ps_v[vb][:], bv_bc[:], ALU.add), waits=wl3, inc=c_dve)
        last_pe_of_tile = c_pe.n
        nvlast = c_dve.n
        for p in range(NPAIR):
            E("pool", lambda e, p=p, t=t: e.dma_start(
                out=v_scr[p, t * TT:(t + 1) * TT, :].rearrange("(tc q) d -> q tc d", q=128),
                in_=vtile[:, :, p * 128:(p + 1) * 128]),
                waits=[(c_dve, nvlast)] if p == 0 else (), inc=c_st)
        vt_written = c_st.n

    b1_scr_written = c_st.n

    if stages <= 1:
        return finish_early()

    ps_q_rmax = list(ps_k_rmax)
    ps_q_evac = list(ps_k_evac)
    for tq in range(nq):
        off, w = btiles[tq]
        sl = tq % 2
        napl = ln_tile(xT_own[:, off:off + w], w, xf[sl], xf_free[sl],
                       last_pe_of_tile, last_pe_of_tile, ps_mean, ps_msq)
        xf_free[sl] = napl
        for p in range(NPAIR):
            qb = p % 2
            wl = ([(c_dve, napl)] if p <= 1 else []) + \
                 [(c_act, ps_q_evac[qb]), (c_dve, ps_q_rmax[qb])]
            nqmm = None
            for g in range(G):
                nqmm = E("pe", lambda e, g=g, p=p, qb=qb, w=w: e.matmul(
                    ps_k[qb][:, 0:w], Wq_bf[:, g, p * 128:(p + 1) * 128], ln1x[:, g, 0:w],
                    start=(g == 0), stop=(g == G - 1), skip_group_check=True),
                    waits=wl if g == 0 else (), inc=c_pe if g == G - 1 else None)
            if w < 32:
                nqmm = drain2(lambda: ps_mean[0:1, 320:384])
            nqev = E("act", lambda e, p=p, qb=qb, off=off, w=w: e.activation(
                qT[:, p, off:off + w], ps_k[qb][:, 0:w], AF.Identity, bias=vecp[:, p:p + 1]),
                waits=[(c_pe, nqmm)], inc=c_act)
            ps_q_evac[qb] = nqev
            nq2 = E("dve", lambda e, p=p, off=off, w=w: e.tensor_tensor(
                k2tmp[:, 0:w], qT[:, p, off:off + w], qT[:, p, off:off + w], ALU.mult),
                waits=[(c_act, nqev)], inc=c_dve)
            nqmx = E("pe", lambda e, qb=qb, w=w: e.matmul(
                ps_k[qb][:, 0:w], ones_bf[:], k2tmp[:, 0:w], start=True, stop=True,
                skip_group_check=True), waits=[(c_dve, nq2)], inc=c_pe)
            if w < 32:
                nqmx = drain2(lambda: ps_mean[0:1, 320:384])
            ps_q_rmax[qb] = E("dve", lambda e, p=p, tq=tq, qb=qb, w=w: e.reduce_max(
                qmax2[:, p * nq + tq:p * nq + tq + 1], ps_k[qb][:, 0:w], axis=AX),
                waits=[(c_pe, nqmx)], inc=c_dve)
        last_pe_of_tile = c_pe.n
    b2_q_done = c_act.n
    b2_dve_done = c_dve.n
    ps_stack.close()

    if debug:
        E("pool", lambda e: e.dma_start(
            out=dbg["wkbf"], in_=Wk_bf[:].rearrange("p g d -> p (g d)")),
            waits=[(c_act, b2_q_done)], inc=c_st)
        E("pool", lambda e: e.dma_start(
            out=dbg["ln1x"], in_=ln1x[:].rearrange("p g t -> p (g t)")),
            waits=[(c_act, b2_q_done)], inc=c_st)
        for p in range(NPAIR):
            E("pool", lambda e, p=p: e.dma_start(
                out=dbg["qT"][p * 128:(p + 1) * 128, :], in_=qT[:, p, :]),
                waits=[(c_act, b2_q_done)], inc=c_st)
        for p in range(NPAIR):
            E("pool", lambda e, p=p: e.dma_start(
                out=dbg["kT"][p * 128:(p + 1) * 128, :], in_=k_scr[p]),
                waits=[(c_st, b1_scr_written)], inc=c_st)
        for p in range(NPAIR):
            E("pool", lambda e, p=p: e.dma_start(
                out=dbg["v"][:, p * 128:(p + 1) * 128], in_=v_scr[p]),
                waits=[(c_st, b1_scr_written)], inc=c_st)

    if stages <= 2:
        return finish_early()

    cs = ExitStack()
    ring = cs.enter_context(nc.psum_tensor("ring", [128, 6, TT], F32))
    ps_av = cs.enter_context(nc.psum_tensor("ps_av", [128, TT], F32))
    ps_den = cs.enter_context(nc.psum_tensor("ps_den", [128, TT], F32))

    def load_pair(p, sl, wait_pe_n):
        E("sp", lambda e, p=p, sl=sl: e.dma_start(out=Kp[sl][:], in_=k_scr[p]),
          waits=[(c_pe, wait_pe_n), (c_st, b1_scr_written)], inc=c_ld)
        for tg in range(8):
            E("sp", lambda e, p=p, sl=sl, tg=tg: e.dma_start(
                out=Vp[sl][:, tg * (nkc // 8):(tg + 1) * (nkc // 8), 0:128],
                in_=v_scr[p, tg * (seq // 8):(tg + 1) * (seq // 8), :]
                .rearrange("(tc q) d -> q tc d", q=128)), inc=c_ld)
        nld = c_ld.n
        non = E("dve", lambda e, sl=sl: e.memset(Vp[sl][:, :, 128:129], 1.0),
                waits=[(c_ld, nld)], inc=c_dve)
        return nld, non

    b2_pe_done = c_pe.n
    kv_state = [load_pair(0, 0, b2_pe_done), load_pair(1, 1, b2_pe_done)]
    av_den_free = 0
    exp_all = 0
    qidx = 0

    npair_run = 1 if "onepair" in CVAR else NPAIR
    for p in range(npair_run):
        sl = p % 2
        ld_n, ones_n = kv_state[sl]
        Kb, Vb = Kp[sl], Vp[sl]
        E("dve", lambda e, p=p: e.reduce_max(
            mvec[:, 6:7], kmax2s[:, p * nb1:(p + 1) * nb1], axis=AX),
            waits=[(c_dve, b2_dve_done)], inc=c_dve)
        for tq in range(nq):
            w, off = qw[tq], qoff[tq]
            mcol = qidx % 4
            E("dve", lambda e, p=p, tq=tq: e.tensor_tensor(
                mvec[:, 7:8], qmax2[:, p * nq + tq:p * nq + tq + 1], mvec[:, 6:7],
                ALU.add), inc=c_dve)
            nmv = E("dve", lambda e, mcol=mcol: e.tensor_scalar(
                mvec[:, mcol:mcol + 1], mvec[:, 7:8], -1.0 / 16.0, -1.0,
                ALU.mult, ALU.add),
                waits=[(c_act, exp_all if qidx >= 4 else 0)], inc=c_dve)
            bias_ap = consts[:, 1:2] if "constbias" in CVAR else mvec[:, mcol:mcol + 1]

            if w != TT and "nomini" in CVAR:
                exp_all = c_act.n
                qidx += 1
                continue
            if w == TT:
                nblk = 2 * nkc
                ngrp = (nblk + 2) // 3
                grp_exp = [0] * ngrp
                grp_sc = [0] * ngrp
                grp_av = [0] * ngrp

                def emit_sgroup(gi):
                    lo = gi * 3
                    hi = min(lo + 3, nblk)
                    n_ = None
                    for b in range(lo, hi):
                        kc, h = b // 2, b % 2
                        slot = b % 6
                        waits = []
                        if b == 0:
                            waits = [(c_ld, ld_n), (c_dve, nmv), (c_act, b2_q_done),
                                     (c_act, exp_all), (c_dve, av_den_free)]
                        if b >= 6:
                            waits.append((c_act, grp_exp[(b - 6) // 3]))
                        n_ = E("pe", lambda e, kc=kc, h=h, slot=slot, Kb=Kb, p=p,
                               off=off, w=w: e.matmul(
                            ring[:, slot, 0:w],
                            Kb[h * 64:(h + 1) * 64, kc * 128:(kc + 1) * 128],
                            qT[h * 64:(h + 1) * 64, p, off:off + w],
                            start=True, stop=True, tile_position=(h * 64, 0),
                            skip_group_check=True), waits=waits,
                            inc=c_pe if b == hi - 1 else None)
                    grp_sc[gi] = n_

                def emit_exp(gi):
                    lo = gi * 3
                    cnt = min(lo + 3, nblk) - lo
                    sl0 = lo % 6
                    waits = [(c_pe, grp_sc[gi])]
                    if gi >= 2:
                        waits.append((c_pe, grp_av[gi - 2]))
                    if "noexp" in CVAR:
                        grp_exp[gi] = E("act", lambda e: e.nop(), waits=waits, inc=c_act)
                    elif "expsb" in CVAR:
                        grp_exp[gi] = E("act", lambda e, sl0=sl0, cnt=cnt, w=w,
                                        ba=bias_ap: e.activation(
                            pring[:, sl0:sl0 + cnt, 0:w], pring[:, sl0:sl0 + cnt, 0:w],
                            AF.Exp, bias=ba, scale=0.125), waits=waits, inc=c_act)
                    else:
                        grp_exp[gi] = E("act", lambda e, sl0=sl0, cnt=cnt, w=w,
                                        ba=bias_ap: e.activation(
                            pring[:, sl0:sl0 + cnt, 0:w], ring[:, sl0:sl0 + cnt, 0:w],
                            AF.Exp, bias=ba, scale=0.125), waits=waits, inc=c_act)

                def emit_avgroup(gi):
                    lo = gi * 3
                    hi = min(lo + 3, nblk)
                    nav = None
                    for b in range(lo, hi):
                        kc, h = b // 2, b % 2
                        slot = b % 6
                        waits = [(c_act, grp_exp[gi])] if b == lo else []
                        if b == 0:
                            waits.append((c_dve, av_den_free))
                        if "noav" in CVAR:
                            if waits:
                                E("pe", lambda e: e.nop(), waits=waits)
                            continue
                        E("pe", lambda e, kc=kc, h=h, slot=slot, Vb=Vb, w=w: e.matmul(
                            ps_av[h * 64:(h + 1) * 64, 0:w],
                            Vb[:, kc, h * 64:(h + 1) * 64],
                            pring[:, slot, 0:w],
                            start=(kc == 0), stop=(kc == nkc - 1),
                            tile_position=(0, h * 64), skip_group_check=True),
                            waits=waits)
                    if "noden" in CVAR:
                        nav = E("pe", lambda e: e.nop(), inc=c_pe)
                    else:
                        for b in range(lo, hi):
                            kc, h = b // 2, b % 2
                            slot = b % 6
                            nav = E("pe", lambda e, kc=kc, h=h, slot=slot, w=w: e.matmul(
                                ps_den[h * 64:(h + 1) * 64, 0:w],
                                ones_bf[:, 0:64],
                                pring[:, slot, 0:w],
                                start=(kc == 0), stop=(kc == nkc - 1),
                                tile_position=(0, h * 64), skip_group_check=True),
                                inc=c_pe if b == hi - 1 else None)
                    grp_av[gi] = nav

                for gi in range(ngrp + 2):
                    if gi < ngrp:
                        emit_sgroup(gi)
                    if gi - 2 >= 0 and gi - 2 < ngrp:
                        emit_avgroup(gi - 2)
                    if gi < ngrp:
                        emit_exp(gi)
                n_att = grp_av[ngrp - 1]
                if "nofin" in CVAR:
                    av_den_free = E("dve", lambda e: e.engine_nop(),
                                    waits=[(c_pe, n_att)], inc=c_dve)
                else:
                    E("act", lambda e, w=w: e.activation(stat4[:, 0:w], ps_den[:, 0:w],
                                                         AF.Ln),
                      waits=[(c_pe, n_att)], inc=c_act)
                    nrec = E("act", lambda e, w=w: e.activation(
                        rec_s[:, 0:w], stat4[:, 0:w], AF.Exp, scale=-1.0), inc=c_act)
                    av_den_free = E("dve", lambda e, p=p, off=off, w=w: e.tensor_mul(
                        attno[:, p, off:off + w], ps_av[:, 0:w], rec_s[:, 0:w]),
                        waits=[(c_act, nrec)], inc=c_dve)
            else:
                waits0 = [(c_ld, ld_n), (c_dve, nmv), (c_dve, ones_n),
                          (c_act, exp_all), (c_dve, av_den_free)]
                nsc = None
                for kc in range(nkc):
                    for h in range(2):
                        nsc = E("pe", lambda e, kc=kc, h=h, Kb=Kb, p=p, off=off: e.matmul(
                            ring[:, h, kc:kc + 1],
                            Kb[h * 64:(h + 1) * 64, kc * 128:(kc + 1) * 128],
                            qT[h * 64:(h + 1) * 64, p, off:off + 1],
                            start=True, stop=True, tile_position=(h * 64, 0),
                            skip_group_check=True),
                            waits=waits0 if (kc == 0 and h == 0) else (),
                            inc=c_pe if (kc == nkc - 1 and h == 1) else None)
                nsc = drain2(lambda: ring[0:1, 5, 320:384])
                nex = E("act", lambda e, ba=bias_ap: e.activation(
                    pring[:, 0:2, 0:nkc], ring[:, 0:2, 0:nkc], AF.Exp,
                    bias=ba, scale=0.125), waits=[(c_pe, nsc)], inc=c_act)
                nav = None
                for kc in range(nkc):
                    for h in range(2):
                        tgt = ps_av if h == 0 else ps_den
                        nav = E("pe", lambda e, kc=kc, h=h, Vb=Vb, tgt=tgt: e.matmul(
                            tgt[0:1, 0:129],
                            pring[:, h, kc:kc + 1],
                            Vb[:, kc, 0:129],
                            start=(kc == 0), stop=(kc == nkc - 1),
                            skip_group_check=True),
                            waits=[(c_act, nex)] if (kc == 0 and h == 0) else (),
                            inc=c_pe if (kc == nkc - 1 and h == 1) else None)
                nav = drain2(lambda: ring[0:1, 5, 320:384])
                nrow = None
                for h in range(2):
                    tgt = ps_av if h == 0 else ps_den
                    E("dve", lambda e, h=h, tgt=tgt: e.tensor_copy(
                        pring[0:1, 3, h * 64:(h + 1) * 64],
                        tgt[0:1, h * 64:h * 64 + 64]),
                        waits=[(c_pe, nav)] if h == 0 else (), inc=c_dve)
                    nrow = E("dve", lambda e, h=h, tgt=tgt: e.tensor_copy(
                        rec_s[0:1, h * 64:(h + 1) * 64],
                        tgt[0:1, 128:129].broadcast_to([1, 64])), inc=c_dve)
                E("pe", lambda e: e.matmul(
                    ring[:, 3, 0:1], rec_s[0:1, 0:128], ones_f[0:1, 0:1],
                    start=True, stop=True, skip_group_check=True),
                    waits=[(c_dve, nrow)], inc=c_pe)
                E("pe", lambda e: e.matmul(
                    ring[:, 2, 0:1], pring[0:1, 3, 0:128], ones_bf[0:1, 0:1],
                    start=True, stop=True, skip_group_check=True), inc=c_pe)
                ntr2 = E("pe", lambda e: e.matmul(
                    ring[:, 3, 0:1], rec_s[0:1, 0:128], ones_f[0:1, 0:1],
                    start=True, stop=True, skip_group_check=True), inc=c_pe)
                E("act", lambda e: e.activation(stat4[:, 0:1], ring[:, 3, 0:1], AF.Ln),
                  waits=[(c_pe, ntr2)], inc=c_act)
                nrec_m = E("act", lambda e: e.activation(
                    rec_s[:, 256:257], stat4[:, 0:1], AF.Exp, scale=-1.0), inc=c_act)
                if debug:
                    E("pool", lambda e, p=p: e.dma_start(
                        out=dbg["pmini"][:, p * 128:p * 128 + 2 * nkc]
                        .rearrange("q (a b) -> q a b", b=nkc),
                        in_=pring[:, 0:2, 0:nkc]),
                        waits=[(c_pe, nav)], inc=c_st)
                    E("dve", lambda e, p=p: e.tensor_copy(
                        rec_s[:, 300 + 0:300 + 2], ps_den[:, 0:2]), inc=c_dve)
                    E("dve", lambda e, p=p: e.tensor_copy(
                        rec_s[:, 302:303], rec_s[:, 256:257]), inc=c_dve)
                    nmd = E("dve", lambda e, p=p, Vb=Vb: e.tensor_copy(
                        rec_s[:, 303:304], Vb[:, 0:2, 128].rearrange("p a -> p a")[:, 0:1]),
                        inc=c_dve)
                    E("pool", lambda e, p=p: e.dma_start(
                        out=dbg["mini"][:, p * 4:p * 4 + 4], in_=rec_s[:, 300:304]),
                        waits=[(c_dve, nmd)], inc=c_st)
                av_den_free = E("dve", lambda e, p=p, off=off: e.tensor_mul(
                    attno[:, p, off:off + 1], ring[:, 2, 0:1], rec_s[:, 256:257]),
                    waits=[(c_act, nrec_m)], inc=c_dve)
            exp_all = c_act.n
            qidx += 1
        if p + 2 < npair_run:
            kv_state[sl] = load_pair(p + 2, sl, c_pe.n)
    attn_done_dve = c_dve.n
    cs.close()

    if debug:
        for p in range(NPAIR):
            E("pool", lambda e, p=p: e.dma_start(
                out=dbg["attn"][p * 128:(p + 1) * 128, :], in_=attno[:, p, :]),
                waits=[(c_dve, attn_done_dve)], inc=c_st)

    if stages <= 3:
        return finish_early()

    n_wo = cast_weight(W_in["Wo"], Wo_bf, wcast[:].rearrange("p a t -> p (a t)"),
                       wait_dve=b2_dve_done)

    ds = ExitStack()
    ps_wo = [ds.enter_context(nc.psum_tensor(f"ps_wo{i}", [128, TT], F32))
             for i in range(2)]
    ps_sm = ds.enter_context(nc.psum_tensor("ps_sm", [128, TT], F32))
    ps_sq2 = ds.enter_context(nc.psum_tensor("ps_sq2", [128, TT], F32))
    ps_k2f = [ds.enter_context(nc.psum_tensor(f"ps_k2f{i}", [128, TT], F32))
              for i in range(2)]
    ps_r = ds.enter_context(nc.psum_tensor("ps_r", [128, TT], F32))
    ps_kv = ds.enter_context(nc.psum_tensor("ps_kv", [128, TT], F32))

    xres_free = [0, 0]
    ps_wo_free = [0, 0]
    for jt in range(nq):
        off, w = btiles[jt]
        sl = jt % 2
        nxl = E("sp", lambda e, off=off, w=w, sl=sl: e.dma_start(
            out=xres[sl][:, :, 0:w],
            in_=xT_own[:, off:off + w].rearrange("(g p) t -> p g t", p=128),
            allow_slow_non_contiguous=(w < 32)),
            waits=[(c_dve, xres_free[sl]), (c_dve, n_wo)], inc=c_ld)
        for oc in range(G):
            ob = oc % 2
            wl = [(c_dve, ps_wo_free[ob])]
            if jt == 0 and oc == 0:
                wl += [(c_dve, n_wo), (c_dve, attn_done_dve)]
            nwm = None
            for dp in range(G):
                nwm = E("pe", lambda e, dp=dp, oc=oc, ob=ob, off=off, w=w: e.matmul(
                    ps_wo[ob][:, 0:w], Wo_bf[:, dp, oc * 128:(oc + 1) * 128],
                    attno[:, dp, off:off + w],
                    start=(dp == 0), stop=(dp == G - 1), skip_group_check=True),
                    waits=wl if dp == 0 else (), inc=c_pe if dp == G - 1 else None)
            if w < 32:
                nwm = drain2(lambda: ps_sq2[0:1, 320:384])
            ps_wo_free[ob] = E("dve", lambda e, oc=oc, ob=ob, off=off, w=w, sl=sl:
                               e.scalar_tensor_tensor(
                postattn[:, oc, off:off + w], ps_wo[ob][:, 0:w],
                vecp[:, 24 + oc:25 + oc], xres[sl][:, oc, 0:w], ALU.add, ALU.add),
                waits=[(c_pe, nwm), (c_ld, nxl)], inc=c_dve)
        xres_free[sl] = c_dve.n
    d1_done = c_dve.n

    if debug:
        for oc in range(G):
            E("pool", lambda e, oc=oc: e.dma_start(
                out=dbg["postattn"][oc * 128:(oc + 1) * 128, :],
                in_=postattn[:, oc, :]), waits=[(c_dve, d1_done)], inc=c_st)

    if stages <= 4:
        return finish_early()

    ln2_flat = ln2t[:].rearrange("p g t -> p (g t)")[:, 0:4 * D]
    n_wr = cast_weight(W_in["Wr_t"], Wr_bf, ln2_flat, wait_dve=d1_done)
    n_wk2 = cast_weight(W_in["Wk2_t"], Wk2_bf, ln2_flat, wait_dve=n_wr)
    n_wv2 = cast_weight(W_in["Wv2_t"], Wv2_bf, ln2_flat, wait_dve=n_wk2)

    d2_pe_prev = 0
    for jt in range(nq):
        off, w = btiles[jt]
        ncst = E("dve", lambda e, off=off, w=w: e.tensor_copy(
            pa_bf[:, :, 0:w], postattn[:, :, off:off + w]),
            waits=[(c_dve, n_wv2 if jt == 0 else 0), (c_pe, d2_pe_prev)], inc=c_dve)
        nsq = E("dve", lambda e, off=off, w=w: e.tensor_tensor(
            x2d_bf[:, :, 0:w], postattn[:, :, off:off + w],
            postattn[:, :, off:off + w], ALU.mult), inc=c_dve)
        nm1 = nm2 = None
        for g in range(G):
            nm1 = E("pe", lambda e, g=g, w=w: e.matmul(
                ps_sm[:, 0:w], ones_bf[:], pa_bf[:, g, 0:w],
                start=(g == 0), stop=(g == G - 1), skip_group_check=True),
                waits=[(c_dve, ncst)] if g == 0 else (),
                inc=c_pe if g == G - 1 else None)
        for g in range(G):
            nm2 = E("pe", lambda e, g=g, w=w: e.matmul(
                ps_sq2[:, 0:w], ones_bf[:], x2d_bf[:, g, 0:w],
                start=(g == 0), stop=(g == G - 1), skip_group_check=True),
                waits=[(c_dve, nsq)] if g == 0 else (),
                inc=c_pe if g == G - 1 else None)
        if w < 32:
            nm1 = nm2 = drain2(lambda: ps_wo[0][0:1, 320:384])
        d2_pe_prev = nm2
        E("dve", lambda e, off=off, w=w: e.tensor_scalar_mul(
            m_all[:, off:off + w], ps_sm[:, 0:w], 1.0 / D),
            waits=[(c_pe, nm1)], inc=c_dve)
        E("dve", lambda e, off=off, w=w: e.tensor_mul(
            stat4[:, 0:w], m_all[:, off:off + w], m_all[:, off:off + w]), inc=c_dve)
        nvar = E("dve", lambda e, off=off, w=w: e.scalar_tensor_tensor(
            var2[:, 0:w], ps_sq2[:, 0:w], 1.0 / D, stat4[:, 0:w],
            ALU.mult, ALU.subtract), waits=[(c_pe, nm2)], inc=c_dve)
        E("act", lambda e, w=w: e.activation(stat4[:, 0:w], var2[:, 0:w], AF.Ln,
                                             bias=consts[:, 0:1]),
          waits=[(c_dve, nvar)], inc=c_act)
        E("act", lambda e, off=off, w=w: e.activation(
            rstd_all[:, off:off + w], stat4[:, 0:w], AF.Exp, scale=-0.5), inc=c_act)
    d2_done = c_act.n

    if debug:
        E("pool", lambda e: e.dma_start(out=dbg["m_all"], in_=m_all[:]),
          waits=[(c_act, d2_done)], inc=c_st)
        E("pool", lambda e: e.dma_start(out=dbg["rstd_all"], in_=rstd_all[:]),
          waits=[(c_act, d2_done)], inc=c_st)

    if stages <= 5:
        return finish_early()

    nffn = own // TT
    out_free = [0, 0]
    ps_k2_free = [0, 0]
    sig_free = [0, 0]
    for jt in range(nffn):
        c0 = 1 + jt * TT
        n1 = E("dve", lambda e, c0=c0: e.tensor_tensor(
            ln2t[:], postattn[:, :, c0 - 1:c0 + TT],
            bc(m_all[:, c0 - 1:c0 + TT], G), ALU.subtract),
            waits=[(c_act, d2_done)], inc=c_dve)
        nln2 = E("dve", lambda e, c0=c0: e.tensor_tensor(
            ln2t[:], ln2t[:], bc(rstd_all[:, c0 - 1:c0 + TT], G), ALU.mult), inc=c_dve)
        if jt == 0:
            nln2 = E("dve", lambda e: e.tensor_tensor(
                ln2t[:, :, 0:1], ln2t[:, :, 0:1],
                mask_s[:].unsqueeze(1).broadcast_to([128, G, 1]), ALU.mult),
                waits=[(c_ld, n_mk)], inc=c_dve)
        nxr = nxk = None
        for g in range(G):
            E("dve", lambda e, g=g: e.tensor_tensor(
                mixtmp[:], ln2t[:, g, 1:TT + 1], ln2t[:, g, 0:TT], ALU.subtract),
                inc=c_dve)
            nxr = E("dve", lambda e, g=g: e.scalar_tensor_tensor(
                xr_t[:, g, :], mixtmp[:], vecp[:, 32 + g:33 + g],
                ln2t[:, g, 0:TT], ALU.mult, ALU.add), inc=c_dve)
            nxk = E("dve", lambda e, g=g: e.scalar_tensor_tensor(
                xk_t[:, g, :], mixtmp[:], vecp[:, 40 + g:41 + g],
                ln2t[:, g, 0:TT], ALU.mult, ALU.add), inc=c_dve)
        for dc in range(G):
            kb = dc % 2
            wl = [(c_dve, nxk), (c_dve, ps_k2_free[kb])]
            nkm = None
            for g in range(G):
                nkm = E("pe", lambda e, g=g, dc=dc, kb=kb: e.matmul(
                    ps_k2f[kb][:], Wk2_bf[:, g, dc * 128:(dc + 1) * 128], xk_t[:, g, :],
                    start=(g == 0), stop=(g == G - 1), skip_group_check=True),
                    waits=wl if g == 0 else (), inc=c_pe if g == G - 1 else None)
            E("dve", lambda e, kb=kb: e.tensor_scalar_max(
                mixtmp[:], ps_k2f[kb][:], 0.0), waits=[(c_pe, nkm)], inc=c_dve)
            ps_k2_free[kb] = E("dve", lambda e, dc=dc: e.tensor_tensor(
                kact[:, dc, :], mixtmp[:], mixtmp[:], ALU.mult), inc=c_dve)
        nkact = ps_k2_free[(G - 1) % 2]
        ps_r_free = 0
        ps_kv_free = 0
        for dc in range(G):
            ob = dc % 2
            nrm = None
            for g in range(G):
                nrm = E("pe", lambda e, g=g, dc=dc: e.matmul(
                    ps_r[:], Wr_bf[:, g, dc * 128:(dc + 1) * 128], xr_t[:, g, :],
                    start=(g == 0), stop=(g == G - 1), skip_group_check=True),
                    waits=[(c_act, ps_r_free), (c_dve, nxr)] if g == 0 else (),
                    inc=c_pe if g == G - 1 else None)
            nsig = E("act", lambda e, ob=ob: e.activation(r_ping[ob][:], ps_r[:],
                                                          AF.Sigmoid),
                     waits=[(c_pe, nrm), (c_dve, max(nkact, sig_free[ob]))], inc=c_act)
            ps_r_free = nsig
            nkvm = None
            for g in range(G):
                nkvm = E("pe", lambda e, g=g, dc=dc: e.matmul(
                    ps_kv[:], Wv2_bf[:, g, dc * 128:(dc + 1) * 128], kact[:, g, :],
                    start=(g == 0), stop=(g == G - 1), skip_group_check=True),
                    waits=[(c_dve, max(nkact, ps_kv_free))] if g == 0 else (),
                    inc=c_pe if g == G - 1 else None)
            nmul = E("dve", lambda e, ob=ob: e.tensor_tensor(
                out_t2[ob][:], ps_kv[:], r_ping[ob][:], ALU.mult),
                waits=[(c_pe, nkvm), (c_act, nsig), (c_st, out_free[ob])], inc=c_dve)
            ps_kv_free = nmul
            sig_free[ob] = nmul
            nadd = E("dve", lambda e, ob=ob, dc=dc, c0=c0: e.tensor_tensor(
                out_t2[ob][:], out_t2[ob][:], postattn[:, dc, c0:c0 + TT], ALU.add),
                inc=c_dve)
            out_free[ob] = E("pool", lambda e, ob=ob, dc=dc, jt=jt: e.dma_start(
                out=out_fm[dc * 128:(dc + 1) * 128, jt * TT:(jt + 1) * TT],
                in_=out_t2[ob][:]), waits=[(c_dve, nadd)], inc=c_st)
    ds.close()

    final_st = c_st.n
    E("pool", lambda e: e.nop(), waits=[(c_st, final_st)])

    with nc.Block() as block:
        block.tensor(lambda e: plan.run("pe", e))
        block.scalar(lambda e: plan.run("act", e))
        block.vector(lambda e: plan.run("dve", e))
        block.sync(lambda e: plan.run("sp", e))
        block.gpsimd(lambda e: plan.run("pool", e))
    return nc


def stage_inputs(inputs, seq, own, n_cores=8):
    x = np.asarray(inputs["L0_attn_x"])
    B = x.shape[0]
    per_batch = n_cores // B
    Ws = {}
    for src, dst, tr in (("L0_attn_Wq", "Wq", False), ("L0_attn_Wk", "Wk", False),
                         ("L0_attn_Wv", "Wv", False), ("L0_attn_Wo", "Wo", False),
                         ("L0_ffn_W_r", "Wr_t", True), ("L0_ffn_W_k", "Wk2_t", True),
                         ("L0_ffn_W_v", "Wv2_t", True)):
        w = np.asarray(inputs[src], dtype=np.float32)
        Ws[dst] = np.ascontiguousarray(w.T if tr else w)

    def chunkvec(v):
        return np.ascontiguousarray(np.asarray(v, dtype=np.float32).reshape(G, 128).T)

    vec_pack = np.concatenate([chunkvec(inputs[k]) for k in (
        "L0_attn_bq", "L0_attn_bk", "L0_attn_bv", "L0_attn_bo",
        "L0_ffn_mu_r", "L0_ffn_mu_k")], axis=1)
    bv_row = np.ascontiguousarray(
        np.asarray(inputs["L0_attn_bv"], dtype=np.float32).reshape(1, D))
    in_maps = []
    for c in range(n_cores):
        b = c // per_batch
        own0 = (c % per_batch) * own
        xT_full = np.ascontiguousarray(np.asarray(x[b], dtype=np.float32).T)
        lo = max(own0 - 1, 0)
        sl = np.asarray(x[b][lo:own0 + own], dtype=np.float32)
        if own0 == 0:
            sl = np.concatenate([sl[0:1], sl], axis=0)
        xT_own = np.ascontiguousarray(sl.T)
        maskv = np.full((128, 1), 0.0 if own0 == 0 else 1.0, dtype=np.float32)
        m = {"xT_full": xT_full, "xT_own": xT_own, "vec_pack": vec_pack,
             "bv_row": bv_row, "maskv": maskv}
        m.update(Ws)
        in_maps.append(m)
    return in_maps


def kernel(**inputs):
    x = np.asarray(inputs["L0_attn_x"])
    B, seq, _ = x.shape
    n_cores = 8
    own = seq // (n_cores // B)
    in_maps = stage_inputs(inputs, seq, own)
    nc = build_nc(seq=seq, own=own)
    res = run_bass_kernel_spmd(nc, in_maps, core_ids=list(range(n_cores)))
    out = np.empty((B, seq, D), dtype=np.float32)
    per_batch = n_cores // B
    for c in range(n_cores):
        b = c // per_batch
        own0 = (c % per_batch) * own
        out[b, own0:own0 + own, :] = res.results[c]["out_fm"].T
    return out
